# version 06 (final)
# baseline (speedup 1.0000x reference)
import numpy as np
import ml_dtypes
import concourse.bass as bass
import concourse.mybir as mybir
from concourse.bass_utils import run_bass_kernel_spmd

F32 = mybir.dt.float32
BF16 = mybir.dt.bfloat16
AF = mybir.ActivationFunctionType
ALU = mybir.AluOpType
AX = mybir.AxisListType
BFD = ml_dtypes.bfloat16

D = 4096
H = 128
P = 64
G = 8
KC = 4
DIN = 8192
T = 1024
NTT = 8
NCH = 8
EPS = 1e-5
HPG = 16
CPG = 1024


class Sem:
    __slots__ = ("h", "cnt")

    def __init__(self, h):
        self.h = h
        self.cnt = 0


class Buf:
    __slots__ = ("t", "off", "n", "a")

    def __init__(self, arena, t, off, n):
        self.a, self.t, self.off, self.n = arena, t, off, n

    def s(self, a, b):
        return self.t[:, self.off + a:self.off + b]

    def c(self, i):
        return self.t[:, self.off + i:self.off + i + 1]

    def r(self, a, b):
        return self.t[0:1, self.off + a:self.off + b]

    def p(self, p0, p1, a, b):
        return self.t[p0:p1, self.off + a:self.off + b]

    @property
    def ap(self):
        return self.t[:, self.off:self.off + self.n]

    def free(self):
        self.a.release(self.off, self.n)


class Arena:
    def __init__(self, tensor, n):
        self.t, self.n = tensor, n
        self.fl = [(0, n)]

    def alloc(self, nelem):
        for i, (off, sz) in enumerate(self.fl):
            if sz >= nelem:
                if sz == nelem:
                    del self.fl[i]
                else:
                    self.fl[i] = (off + nelem, sz - nelem)
                return Buf(self, self.t, off, nelem)
        raise MemoryError(f"arena OOM want {nelem} free {self.fl}")

    def release(self, off, n):
        self.fl.append((off, n))
        self.fl.sort()
        out = []
        for o, s in self.fl:
            if out and out[-1][0] + out[-1][1] == o:
                out[-1][1] += s
            else:
                out.append([o, s])
        self.fl = [(o, s) for o, s in out]


class Prog:
    def __init__(self):
        self.q = {k: [] for k in "TVSGY"}

    def emit(self, eng, fn, waits=(), inc=None):
        ws = tuple(waits)

        def cl(e, fn=fn, ws=ws, inc=inc):
            for (sm, v) in ws:
                e.wait_ge(sm.h, v)
            if fn is not None:
                ins = fn(e)
                if inc is not None:
                    ins.then_inc(inc[0].h, inc[1])
        self.q[eng].append(cl)
        if inc is not None:
            inc[0].cnt += inc[1]


def build(debug=False):
    nc = bass.Bass("TRN2", target_bir_lowering=False)

    def din(name, shape, dt):
        return nc.dram_tensor(name, list(shape), dt, kind="ExternalInput").ap()

    x_own = din("x_own", (T, D), F32)
    x_pro = din("x_pro", (T, D), F32)
    w1t = din("w1t", (17, 128, D), F32)
    wht = din("wht", (64, 128, D), F32)
    wgt = din("wgt", (64, 128, D), F32)
    wot = din("wot", (32, 128, DIN), F32)
    wbt = din("wbt", (32, 128, D), F32)
    bft = din("bft", (32, 128, 1024), F32)
    arw = din("arw", (128, 2048), F32)
    rww = din("rww", (128, 512), F32)
    cwB1 = din("cwB1", (128, 16 * KC), F32)
    cbB1 = din("cbB1", (128, 16), F32)
    cwHi = din("cwH", (128, 64 * KC), F32)
    cbHi = din("cbH", (128, 64), F32)
    gnw_i = din("gnw", (128, 64), F32)
    nw_i = din("nww", (128, 32), F32)
    bb_i = din("bbw", (128, 32), F32)
    dtb_i = din("dtb", (128, 1), F32)
    alog_i = din("alog", (128, 1), F32)
    dprow_i = din("dprow", (1, 128), F32)
    flag_i = din("flagc", (128, 1), F32)
    idf_i = din("idf", (128, 128), F32)
    idb_i = din("idb", (128, 128), BF16)
    trif_i = din("trif", (128, 128), F32)
    onesf_i = din("onesf", (1, 128), F32)

    out_t = nc.dram_tensor("out", [T, D], F32, kind="ExternalOutput").ap()

    dk = dict(kind="ExternalOutput") if debug else {}
    d_xnTf = nc.dram_tensor("d_xnTf", [32, 128, T], F32, **dk).ap()
    d_xnTb = nc.dram_tensor("d_xnTb", [32, 128, T], BF16, **dk).ap()
    d_yT = nc.dram_tensor("d_yT", [64, 128, T], BF16, **dk).ap()
    d_sfm = nc.dram_tensor("d_sfm", [32, 128, T], F32, **dk).ap()
    d_ppro = nc.dram_tensor("d_ppro", [G, 128, HPG * P], BF16, **dk).ap()
    d_dbg = nc.dram_tensor("d_dbg", [16, 128, T], F32, **dk).ap() if debug else None

    pr = Prog()
    from contextlib import ExitStack
    es = ExitStack()
    with es:
        E = es.enter_context
        NBF = 76800
        NF3 = 13696
        abf_t = E(nc.sbuf_tensor("abf", [128, NBF], BF16))
        af_t = E(nc.sbuf_tensor("af32", [128, NF3], F32))
        ABF = Arena(abf_t, NBF)
        AFA = Arena(af_t, NF3)

        pb = [E(nc.psum_tensor(f"pb{i}", [128, 512], F32)) for i in range(6)]
        pbt = E(nc.psum_tensor("pbt", [128, 1024], BF16))
        pb7 = E(nc.psum_tensor("pb7", [128, 512], F32))

        qw = Sem(E(nc.semaphore("qw")))
        qio = Sem(E(nc.semaphore("qio")))
        qo = Sem(E(nc.semaphore("qo")))
        spe = Sem(E(nc.semaphore("spe")))
        sact = Sem(E(nc.semaphore("sact")))
        sdve = Sem(E(nc.semaphore("sdve")))
        sgp = Sem(E(nc.semaphore("sgp")))

        def T_(fn, waits=(), inc=None):
            pr.emit("T", fn, waits, inc)

        def V_(fn, waits=(), inc=None):
            pr.emit("V", fn, waits, inc)

        def S_(fn, waits=(), inc=None):
            pr.emit("S", fn, waits, inc)

        def G_(fn, waits=(), inc=None):
            pr.emit("G", fn, waits, inc)

        def Y_(fn, waits=(), inc=None):
            pr.emit("Y", fn, waits, inc)

        def allw():
            return [(spe, spe.cnt), (sact, sact.cnt), (sdve, sdve.cnt),
                    (sgp, sgp.cnt)]

        def VE(fn, waits=()):
            pr.emit("V", fn, list(waits) + [(sdve, sdve.cnt)], (sdve, 1))

        c_idf = AFA.alloc(128)
        c_tri = AFA.alloc(128)
        c_cwB1 = AFA.alloc(16 * KC)
        c_cbB1 = AFA.alloc(16)
        c_cwH = AFA.alloc(64 * KC)
        c_cbH = AFA.alloc(64)
        c_gnw = AFA.alloc(64)
        c_nw = AFA.alloc(32)
        c_bb = AFA.alloc(32)
        c_dtb = AFA.alloc(1)
        c_acol = AFA.alloc(1)
        c_dprow = AFA.alloc(128)
        c_flag = AFA.alloc(1)
        c_ones = AFA.alloc(128)
        c_dpbc = AFA.alloc(128)
        c_rows = AFA.alloc(512)
        c_mrg = AFA.alloc(80 * 3)
        c_idb = ABF.alloc(128)

        for dst, src in [
            (c_idf.ap, idf_i), (c_tri.ap, trif_i), (c_cwB1.ap, cwB1),
            (c_cbB1.ap, cbB1), (c_cwH.ap, cwHi), (c_cbH.ap, cbHi),
            (c_gnw.ap, gnw_i), (c_nw.ap, nw_i), (c_bb.ap, bb_i),
            (c_dtb.ap, dtb_i), (c_acol.ap, alog_i),
            (c_dprow.r(0, 128), dprow_i), (c_flag.ap, flag_i),
            (c_ones.r(0, 128), onesf_i), (c_idb.ap, idb_i),
        ]:
            Y_(lambda y, d=dst, s=src: y.dma_start(d, s), inc=(qio, 16))
        qc0 = qio.cnt
        S_(lambda s: s.activation(c_acol.ap, c_acol.ap, AF.Exp),
           waits=[(qio, qc0)], inc=(sact, 1))
        V_(lambda v: v.tensor_scalar(c_acol.ap, c_acol.ap, -1.0, None, ALU.mult),
           waits=[(sact, sact.cnt)], inc=(sdve, 1))
        T_(lambda t: t.matmul(pb7[:, 0:128], c_ones.r(0, 128), c_dprow.r(0, 128),
                              start=True, stop=True),
           waits=[(qio, qc0)], inc=(spe, 1))
        S_(lambda s: s.activation(c_dpbc.ap, pb7[:, 0:128], AF.Copy),
           waits=[(spe, spe.cnt)], inc=(sact, 1))
        G_(lambda g: g.memset(c_mrg.ap, 0.0), inc=(sgp, 1))

        def emit_pass(xsrc, main):
            hT = ABF.alloc(32 * T)

            xq = [AFA.alloc(1024) for _ in range(2)]
            xn = AFA.alloc(D)
            scr = AFA.alloc(1024)
            cols = AFA.alloc(16)
            rbc = AFA.alloc(128)
            stg = AFA.alloc(128)
            stgb = ABF.alloc(128)

            xq_free = [allw(), allw()]
            for tt in range(NTT):
                for qq in range(4):
                    b = xq[qq % 2]
                    Y_(lambda y, b=b, tt=tt, qq=qq: y.dma_start(
                        b.ap, xsrc[tt * 128:(tt + 1) * 128, qq * 1024:(qq + 1) * 1024]),
                       waits=xq_free[qq % 2], inc=(qio, 16))
                    wq = qio.cnt
                    V_(lambda v, b=b, qq=qq: v.reduce_sum(cols.c(qq), b.ap, axis=AX.X),
                       waits=[(qio, wq)])
                    S_(lambda s, b=b: s.square(scr.ap, b.ap),
                       waits=[(qio, wq), (sdve, sdve.cnt)], inc=(sact, 1))
                    V_(lambda v, qq=qq: v.reduce_sum(cols.c(4 + qq), scr.ap, axis=AX.X),
                       waits=[(sact, sact.cnt)])
                    V_(lambda v, b=b, qq=qq: v.tensor_copy(
                        xn.s(qq * 1024, (qq + 1) * 1024), b.ap), inc=(sdve, 1))
                    xq_free[qq % 2] = [(sdve, sdve.cnt), (sact, sact.cnt)]
                VE(lambda v: v.tensor_tensor(cols.c(0), cols.c(0), cols.c(1), ALU.add))
                VE(lambda v: v.tensor_tensor(cols.c(2), cols.c(2), cols.c(3), ALU.add))
                VE(lambda v: v.tensor_tensor(cols.c(0), cols.c(0), cols.c(2), ALU.add))
                VE(lambda v: v.tensor_scalar(cols.c(0), cols.c(0), 1.0 / D, None, ALU.mult))
                VE(lambda v: v.tensor_tensor(cols.c(4), cols.c(4), cols.c(5), ALU.add))
                VE(lambda v: v.tensor_tensor(cols.c(6), cols.c(6), cols.c(7), ALU.add))
                VE(lambda v: v.tensor_tensor(cols.c(4), cols.c(4), cols.c(6), ALU.add))
                VE(lambda v: v.tensor_scalar(cols.c(4), cols.c(4), 1.0 / D, None, ALU.mult))
                VE(lambda v: v.tensor_tensor(cols.c(1), cols.c(0), cols.c(0), ALU.mult))
                VE(lambda v: v.tensor_tensor(cols.c(4), cols.c(4), cols.c(1), ALU.subtract))
                VE(lambda v: v.tensor_scalar(cols.c(5), cols.c(4), EPS, None, ALU.add))
                S_(lambda s: s.sqrt(cols.c(6), cols.c(5)),
                   waits=[(sdve, sdve.cnt)], inc=(sact, 1))
                VE(lambda v: v.reciprocal(cols.c(7), cols.c(6)), waits=[(sact, sact.cnt)])
                VE(lambda v: v.tensor_scalar(xn.ap, xn.ap, cols.c(0), cols.c(7),
                                             ALU.subtract, ALU.mult))
                VE(lambda v: v.tensor_tensor(cols.c(8), cols.c(4), cols.c(7), ALU.mult))
                VE(lambda v: v.tensor_tensor(cols.c(8), cols.c(8), cols.c(7), ALU.mult))
                VE(lambda v: v.tensor_scalar(cols.c(9), cols.c(8), EPS, None, ALU.add))
                S_(lambda s: s.sqrt(cols.c(10), cols.c(9)),
                   waits=[(sdve, sdve.cnt)], inc=(sact, 1))
                VE(lambda v: v.reciprocal(cols.c(11), cols.c(10)),
                   waits=[(sact, sact.cnt)])
                T_(lambda t: t.matmul(pb7[0:1, 384:512], cols.c(11), c_idf.ap,
                                      start=True, stop=True),
                   waits=[(sdve, sdve.cnt), (sact, sact.cnt)], inc=(spe, 1))
                S_(lambda s: s.activation(c_rows.r(0, 128), pb7[0:1, 384:512], AF.Copy),
                   waits=[(spe, spe.cnt)], inc=(sact, 1))
                T_(lambda t: t.matmul(pb7[:, 384:512], c_ones.r(0, 128),
                                      c_rows.r(0, 128), start=True, stop=True),
                   waits=[(sact, sact.cnt)], inc=(spe, 1))
                S_(lambda s: s.activation(rbc.ap, pb7[:, 384:512], AF.Copy),
                   waits=[(spe, spe.cnt), (sdve, sdve.cnt)], inc=(sact, 1))

                for kb in range(11):
                    kts = list(range(kb * 3, min(32, kb * 3 + 3)))
                    for j, kt in enumerate(kts):
                        T_(lambda t, kt=kt, j=j: t.transpose(
                            pb7[:, j * 128:(j + 1) * 128],
                            xn.s(kt * 128, (kt + 1) * 128), c_idf.ap),
                           waits=([(sact, sact.cnt), (sdve, sdve.cnt)] if j == 0 else []),
                           inc=((spe, 1) if j == len(kts) - 1 else None))
                    wtp = spe.cnt
                    for j, kt in enumerate(kts):
                        if main:
                            S_(lambda s, j=j: s.activation(
                                stg.ap, pb7[:, j * 128:(j + 1) * 128], AF.Copy),
                               waits=[(spe, wtp), (qio, qio.cnt)])
                            S_(lambda s, j=j: s.activation(
                                stgb.ap, pb7[:, j * 128:(j + 1) * 128], AF.Copy),
                               inc=(sact, 1))
                            Y_(lambda y, kt=kt, tt=tt: y.dma_start(
                                d_xnTf[kt, :, tt * 128:(tt + 1) * 128], stg.ap),
                               waits=[(sact, sact.cnt)], inc=(qio, 16))
                            Y_(lambda y, kt=kt, tt=tt: y.dma_start(
                                d_xnTb[kt, :, tt * 128:(tt + 1) * 128], stgb.ap),
                               inc=(qio, 16))
                        V_(lambda v, kt=kt, tt=tt, j=j: v.scalar_tensor_tensor(
                            hT.s(kt * T + tt * 128, kt * T + (tt + 1) * 128),
                            pb7[:, j * 128:(j + 1) * 128], c_nw.c(kt), rbc.ap,
                            ALU.mult, ALU.mult),
                           waits=[(spe, wtp)] + ([(sact, sact.cnt)] if main else []),
                           inc=((sdve, 1) if j == len(kts) - 1 else None))
            for b in xq:
                b.free()
            xn.free(); scr.free(); cols.free(); rbc.free(); stg.free(); stgb.free()
            hT_done = (sdve, sdve.cnt)

            dtT = AFA.alloc(T)
            sc_dt = AFA.alloc(NCH * 128)
            sc_dec = AFA.alloc(NCH * 128)
            sc_ecs = AFA.alloc(NCH * 128) if main else None
            sc_ncs = AFA.alloc(NCH * 128)
            sc_a = AFA.alloc(NCH * 128)
            P_g = AFA.alloc(HPG * P)
            BC = ABF.alloc(16 * T)
            wsl = ABF.alloc(32 * 128)
            ovl = AFA.alloc(3 + 3 * T)
            pre = Buf(AFA, af_t, ovl.off, 3 + T)
            cv = [Buf(AFA, af_t, ovl.off + 3 + T, T),
                  Buf(AFA, af_t, ovl.off + 3 + 2 * T, T)]
            yA = Buf(AFA, af_t, ovl.off, T)
            yB = Buf(AFA, af_t, ovl.off + T, T)
            tp1 = AFA.alloc(512)
            tp2 = AFA.alloc(512)
            bank_free = [(sact, sact.cnt), (sact, sact.cnt)]

            whf_free = [[], []]

            def wstream_mm(src_ap, evac_fn, first_extra=()):
                hf_load = [0, 0]
                for hf in range(2):
                    G_(lambda g, s=src_ap, hf=hf: g.dma_start(
                        wsl.s(hf * 2048, (hf + 1) * 2048),
                        s[:, hf * 2048:(hf + 1) * 2048]),
                       waits=list(whf_free[hf]) + (list(first_extra) if hf == 0 else []),
                       inc=(qw, 16))
                    hf_load[hf] = qw.cnt
                wp = [0, 0]
                for kt in range(32):
                    for nb in range(2):
                        bank = pb[nb]
                        st = (kt == 0)
                        sp = (kt == 31)
                        ws = []
                        if kt == 0:
                            ws = [(qw, hf_load[0]), bank_free[nb]]
                        elif kt == 16 and nb == 0:
                            ws = [(qw, hf_load[1])]
                        mark = (kt == 15 and nb == 1) or sp
                        T_(lambda t, bank=bank, kt=kt, nb=nb, st=st, sp=sp:
                           t.matmul(bank[:, 0:512], wsl.s(kt * 128, (kt + 1) * 128),
                                    hT.s(kt * T + nb * 512, kt * T + nb * 512 + 512),
                                    start=st, stop=sp),
                           waits=ws, inc=((spe, 1) if mark else None))
                        if kt == 15 and nb == 1:
                            whf_free[0] = [(spe, spe.cnt)]
                        if sp:
                            wp[nb] = spe.cnt
                whf_free[1] = [(spe, spe.cnt)]
                evac_fn(0, wp[0])
                evac_fn(1, wp[1])

            def conv_tile(ci, cw, cwo, cb, cbo, silu_dst):
                if main:
                    V_(lambda v, ci=ci: v.tensor_scalar(
                        pre.s(0, 3), c_mrg.s(ci * 3, ci * 3 + 3), c_flag.ap, None,
                        ALU.mult), waits=[(sact, sact.cnt), (sgp, sgp.cnt)])
                else:
                    V_(lambda v, ci=ci: v.tensor_copy(
                        pre.s(0, 3), c_mrg.s(ci * 3, ci * 3 + 3)),
                       waits=[(sact, sact.cnt), (sgp, sgp.cnt)])
                V_(lambda v, cw=cw, o=cwo, cb=cb, ob=cbo: v.tensor_scalar(
                    cv[0].ap, pre.s(0, T), cw.c(o), cb.c(ob), ALU.mult, ALU.add))
                for i in range(1, KC):
                    a, b2 = cv[(i - 1) % 2], cv[i % 2]
                    V_(lambda v, i=i, a=a, b2=b2, cw=cw, o=cwo: v.scalar_tensor_tensor(
                        b2.ap, pre.s(i, i + T), cw.c(o + i), a.ap, ALU.mult, ALU.add),
                       inc=((sdve, 1) if i == KC - 1 else None))
                if not main:
                    G_(lambda g, ci=ci: g.tensor_copy(
                        c_mrg.s(ci * 3, ci * 3 + 3), pre.s(T, T + 3)),
                       waits=[(sact, sact.cnt)], inc=(sgp, 1))
                S_(lambda s, d=silu_dst: s.activation(d, cv[3 % 2].ap, AF.Silu),
                   waits=[(sgp, sgp.cnt), (sdve, sdve.cnt)], inc=(sact, 1))

            for ot in range(17):
                if ot == 0:
                    def dt_evac(nb, wp):
                        bank = pb[nb]
                        S_(lambda s, bank=bank: s.activation(
                            tp1.ap, bank[:, 0:512], AF.Abs, bias=c_dtb.ap),
                           waits=[(spe, wp), (sdve, sdve.cnt)])
                        S_(lambda s: s.activation(tp1.ap, tp1.ap, AF.Exp, scale=-1.0))
                        S_(lambda s: s.activation(tp1.ap, tp1.ap, AF.Ln, bias=1.0))
                        S_(lambda s, bank=bank: s.activation(
                            tp2.ap, bank[:, 0:512], AF.Relu, bias=c_dtb.ap),
                           inc=(sact, 1))
                        bank_free[nb] = (sact, sact.cnt)
                        V_(lambda v, nb=nb: v.tensor_tensor(
                            dtT.s(nb * 512, (nb + 1) * 512), tp1.ap, tp2.ap, ALU.add),
                           waits=[(sact, sact.cnt)], inc=(sdve, 1))
                    wstream_mm(w1t[0], dt_evac,
                               first_extra=[hT_done, (sact, sact.cnt),
                                            (sgp, sgp.cnt), (spe, spe.cnt)])
                else:
                    ci = ot - 1

                    def bc_evac(nb, wp):
                        bank = pb[nb]
                        S_(lambda s, bank=bank, nb=nb: s.activation(
                            pre.s(3 + nb * 512, 3 + (nb + 1) * 512), bank[:, 0:512],
                            AF.Copy), waits=[(spe, wp), (sdve, sdve.cnt)],
                           inc=(sact, 1))
                        bank_free[nb] = (sact, sact.cnt)
                    wstream_mm(w1t[ot], bc_evac)
                    conv_tile(ci, c_cwB1, ci * KC, c_cbB1, ci, BC.s(ci * T, (ci + 1) * T))
            tp1.free(); tp2.free()

            afm = AFA.alloc(128)
            csb = AFA.alloc(128)
            dfm = AFA.alloc(128)
            for c in range(NCH):
                V_(lambda v, c=c: v.tensor_scalar(
                    afm.ap, dtT.s(c * 128, (c + 1) * 128), c_acol.ap, None, ALU.mult),
                   waits=[(sact, sact.cnt), (spe, spe.cnt)], inc=(sdve, 1))
                T_(lambda t: t.transpose(pb7[:, 0:128], afm.ap, c_idf.ap),
                   waits=[(sdve, sdve.cnt)], inc=(spe, 1))
                S_(lambda s: s.activation(afm.ap, pb7[:, 0:128], AF.Copy),
                   waits=[(spe, spe.cnt)], inc=(sact, 1))
                T_(lambda t: t.matmul(pb7[:, 128:256], afm.ap, c_tri.ap,
                                      start=True, stop=True),
                   waits=[(sact, sact.cnt)], inc=(spe, 1))
                S_(lambda s: s.activation(csb.ap, pb7[:, 128:256], AF.Copy),
                   waits=[(spe, spe.cnt)], inc=(sact, 1))
                S_(lambda s: s.activation(dfm.ap, csb.ap, AF.Exp, scale=-1.0,
                                          bias=csb.c(127)),
                   waits=[(sact, sact.cnt)], inc=(sact, 1))
                T_(lambda t: t.transpose(pb7[:, 256:384], dfm.ap, c_idf.ap),
                   waits=[(sact, sact.cnt)], inc=(spe, 1))
                S_(lambda s, c=c: s.activation(
                    sc_dec.s(c * 128, (c + 1) * 128), pb7[:, 256:384], AF.Copy),
                   waits=[(spe, spe.cnt)], inc=(sact, 1))
                T_(lambda t, c=c: t.transpose(
                    pb7[:, 384:512], dtT.s(c * 128, (c + 1) * 128), c_idf.ap),
                   waits=[(sact, sact.cnt)], inc=(spe, 1))
                S_(lambda s, c=c: s.activation(
                    sc_dt.s(c * 128, (c + 1) * 128), pb7[:, 384:512], AF.Copy),
                   waits=[(spe, spe.cnt)], inc=(sact, 1))
                T_(lambda t: t.transpose(pb7[:, 0:128], csb.ap, c_idf.ap),
                   waits=[(sact, sact.cnt)], inc=(spe, 1))
                S_(lambda s, c=c: s.activation(
                    sc_ncs.s(c * 128, (c + 1) * 128), pb7[:, 0:128], AF.Copy,
                    scale=-1.0), waits=[(spe, spe.cnt)])
                if main:
                    S_(lambda s, c=c: s.activation(
                        sc_ecs.s(c * 128, (c + 1) * 128), pb7[:, 0:128], AF.Exp))
                S_(lambda s: s.activation(dfm.c(0), csb.c(127), AF.Exp),
                   inc=(sact, 1))
                T_(lambda t: t.matmul(pb7[0:1, 128:256], dfm.c(0), c_idf.ap,
                                      start=True, stop=True),
                   waits=[(sact, sact.cnt)], inc=(spe, 1))
                S_(lambda s: s.activation(c_rows.r(128, 256), pb7[0:1, 128:256],
                                          AF.Copy),
                   waits=[(spe, spe.cnt)], inc=(sact, 1))
                T_(lambda t: t.matmul(pb7[:, 128:256], c_ones.r(0, 128),
                                      c_rows.r(128, 256), start=True, stop=True),
                   waits=[(sact, sact.cnt)], inc=(spe, 1))
                S_(lambda s, c=c: s.activation(
                    sc_a.s(c * 128, (c + 1) * 128), pb7[:, 128:256], AF.Copy),
                   waits=[(spe, spe.cnt)], inc=(sact, 1))
            afm.free(); csb.free(); dfm.free()

            if debug and not main:
                for i, bsrc in enumerate([dtT, sc_dec, sc_a, sc_dt]):
                    Y_(lambda y, i=i, b=bsrc: y.dma_start(d_dbg[i], b.s(0, 1024)),
                       waits=[(sact, sact.cnt), (sdve, sdve.cnt), (spe, spe.cnt)],
                       inc=(qio, 16))

            hidtm = ABF.alloc(NCH * CPG)
            gfm = ABF.alloc(8 * T) if main else None
            hfm = ABF.alloc(T)
            P_bf = ABF.alloc(HPG * P)
            cbe = [ABF.alloc(128) for _ in range(4)]
            btm = ABF.alloc(128)
            hdt = ABF.alloc(CPG) if main else None
            hdtd = ABF.alloc(CPG)
            yst = ABF.alloc(8 * 128) if main else None
            cbm = AFA.alloc(128)
            esb = AFA.alloc(4 * 128)
            esb2 = AFA.alloc(4 * 128)
            dtdc = AFA.alloc(128) if not main else None
            gcols = AFA.alloc(8)
            pbt_free = (sdve, sdve.cnt)
            y_banks_free = (sdve, sdve.cnt)
            st_free = (sdve, sdve.cnt)
            pb4_free = (sdve, sdve.cnt)
            yst_free = (qio, qio.cnt)

            for gi in range(G):
                for ot in range(8):
                    got = gi * 8 + ot

                    def hid_evac(nb, wp):
                        bank = pb[nb]
                        S_(lambda s, bank=bank, nb=nb: s.activation(
                            pre.s(3 + nb * 512, 3 + (nb + 1) * 512), bank[:, 0:512],
                            AF.Copy), waits=[(spe, wp), (sdve, sdve.cnt)],
                           inc=(sact, 1))
                        bank_free[nb] = (sact, sact.cnt)
                    wstream_mm(wht[got], hid_evac)
                    conv_tile(16 + got, c_cwH, got * KC, c_cbH, got, hfm.ap)
                    for c in range(NCH):
                        T_(lambda t, c=c: t.transpose(
                            pbt[:, c * 128:(c + 1) * 128],
                            hfm.s(c * 128, (c + 1) * 128), c_idb.ap),
                           waits=([(sact, sact.cnt), pbt_free] if c == 0 else []),
                           inc=((spe, 1) if c == NCH - 1 else None))
                    wtp = spe.cnt
                    for c in range(NCH):
                        V_(lambda v, c=c, ot=ot: v.tensor_copy(
                            hidtm.s(c * CPG + ot * 128, c * CPG + (ot + 1) * 128),
                            pbt[:, c * 128:(c + 1) * 128]),
                           waits=([(spe, wtp)] if c == 0 else []),
                           inc=((sdve, 1) if c == NCH - 1 else None))
                    pbt_free = (sdve, sdve.cnt)
                if main:
                    for ot in range(8):
                        got = gi * 8 + ot

                        def gate_evac(nb, wp, ot=ot):
                            bank = pb[nb]
                            S_(lambda s, bank=bank, nb=nb, ot=ot: s.activation(
                                gfm.s(ot * T + nb * 512, ot * T + (nb + 1) * 512),
                                bank[:, 0:512], AF.Silu),
                               waits=[(spe, wp), (sdve, sdve.cnt)], inc=(sact, 1))
                            bank_free[nb] = (sact, sact.cnt)
                        wstream_mm(wgt[got], gate_evac)

                if debug and not main and gi == 0:
                    G_(lambda g: g.dma_start(d_dbg[4], BC.s(0, 1024)),
                       waits=[(sact, sact.cnt), (sdve, sdve.cnt)], inc=(qw, 16))
                    G_(lambda g: g.dma_start(d_dbg[5], hidtm.s(0, 1024)),
                       inc=(qw, 16))
                    G_(lambda g: g.dma_start(d_dbg[6], hT.s(0, 1024)),
                       inc=(qw, 16))
                    G_(lambda g: g.dma_start(d_dbg[7], hdtd.s(0, 1024) if False else BC.s(8 * T, 8 * T + 1024)),
                       inc=(qw, 16))
                if main:
                    Y_(lambda y, gi=gi: y.dma_start(P_bf.ap, d_ppro[gi]),
                       waits=[(qio, qio.cnt), (spe, spe.cnt)], inc=(qio, 16))
                    V_(lambda v: v.tensor_scalar(P_g.ap, P_bf.ap, c_flag.ap, None,
                                                 ALU.mult),
                       waits=[(qio, qio.cnt)], inc=(sdve, 1))
                else:
                    V_(lambda v: v.memset(P_g.ap, 0.0), inc=(sdve, 1))

                T_(None, waits=[(sact, sact.cnt), (sdve, sdve.cnt)])

                for c in range(NCH):
                    cw = c * CPG
                    if main:
                        for hh in range(HPG):
                            h = gi * HPG + hh
                            V_(lambda v, hh=hh, h=h, cw=cw, c=c: v.tensor_scalar(
                                hdt.s(hh * P, (hh + 1) * P),
                                hidtm.s(cw + hh * P, cw + hh * P + P),
                                sc_dt.c(c * 128 + h), None, ALU.mult),
                               inc=((sdve, 1) if hh == HPG - 1 else None))
                        for hh in range(HPG):
                            h = gi * HPG + hh
                            V_(lambda v, hh=hh, h=h, c=c: v.tensor_scalar(
                                hdtd.s(hh * P, (hh + 1) * P),
                                hdt.s(hh * P, (hh + 1) * P),
                                sc_dec.c(c * 128 + h), None, ALU.mult),
                               waits=([(sdve, sdve.cnt)] if hh == 0 else []),
                               inc=((sdve, 1) if hh == HPG - 1 else None))
                    else:
                        VE(lambda v, c=c: v.tensor_tensor(
                            dtdc.ap, sc_dt.s(c * 128, (c + 1) * 128),
                            sc_dec.s(c * 128, (c + 1) * 128), ALU.mult))
                        for hh in range(HPG):
                            h = gi * HPG + hh
                            V_(lambda v, hh=hh, h=h, cw=cw: v.tensor_scalar(
                                hdtd.s(hh * P, (hh + 1) * P),
                                hidtm.s(cw + hh * P, cw + hh * P + P),
                                dtdc.c(h), None, ALU.mult),
                               inc=((sdve, 1) if hh == HPG - 1 else None))
                    wdt = sdve.cnt

                    if main:
                        S_(lambda s: s.activation(P_bf.ap, P_g.ap, AF.Copy),
                           waits=[(sdve, sdve.cnt)], inc=(sact, 1))
                        wps = sact.cnt
                        for half in range(2):
                            T_(lambda t, c=c, gi=gi, half=half: t.matmul(
                                pb[half][:, 0:512],
                                BC.s((8 + gi) * T + c * 128, (8 + gi) * T + (c + 1) * 128),
                                P_bf.s(half * 512, (half + 1) * 512),
                                start=True, stop=True),
                               waits=([(sact, wps), y_banks_free] if half == 0 else []),
                               inc=((spe, 1) if half == 1 else None))
                        T_(lambda t, c=c, gi=gi: t.matmul(
                            pb[4][:, 384:512],
                            BC.s(gi * T + c * 128, gi * T + (c + 1) * 128),
                            BC.s((8 + gi) * T + c * 128, (8 + gi) * T + (c + 1) * 128),
                            start=True, stop=True),
                           waits=[pb4_free], inc=(spe, 1))
                        wcb = spe.cnt
                        V_(lambda v: v.tensor_tensor(cbm.ap, pb[4][:, 384:512], c_tri.ap,
                                                     ALU.mult),
                           waits=[(spe, wcb)], inc=(sdve, 1))
                        wcbm = sdve.cnt
                        for hb in range(4):
                            hhs = [hb * 4 + j for j in range(4)]
                            for j, hh in enumerate(hhs):
                                h = gi * HPG + hh
                                T_(lambda t, c=c, h=h, j=j: t.matmul(
                                    pb[4][0:1, j * 128:(j + 1) * 128],
                                    sc_ncs.c(c * 128 + h), c_idf.ap,
                                    start=True, stop=True),
                                   waits=([(sdve, sdve.cnt)] if j == 0 else []),
                                   inc=((spe, 1) if j == 3 else None))
                            wrows = spe.cnt
                            for j in range(4):
                                S_(lambda s, j=j: s.activation(
                                    c_rows.r(j * 128, (j + 1) * 128),
                                    pb[4][0:1, j * 128:(j + 1) * 128], AF.Copy),
                                   waits=([(spe, wrows)] if j == 0 else []),
                                   inc=((sact, 1) if j == 3 else None))
                            wrevac = sact.cnt
                            for j in range(4):
                                T_(lambda t, j=j: t.matmul(
                                    pb[4][:, j * 128:(j + 1) * 128],
                                    c_ones.r(0, 128), c_rows.r(j * 128, (j + 1) * 128),
                                    start=True, stop=True),
                                   waits=([(sact, wrevac)] if j == 0 else []),
                                   inc=((spe, 1) if j == 3 else None))
                            wv = spe.cnt
                            for j, hh in enumerate(hhs):
                                h = gi * HPG + hh
                                V_(lambda v, c=c, h=h, j=j: v.tensor_scalar(
                                    esb.s(j * 128, (j + 1) * 128),
                                    pb[4][:, j * 128:(j + 1) * 128],
                                    sc_ncs.c(c * 128 + h), 0.0, ALU.subtract, ALU.max),
                                   waits=([(spe, wv)] if j == 0 else []),
                                   inc=((sdve, 1) if j == 3 else None))
                            for j, hh in enumerate(hhs):
                                S_(lambda s, j=j: s.activation(
                                    esb2.s(j * 128, (j + 1) * 128),
                                    esb.s(j * 128, (j + 1) * 128), AF.Exp, scale=-1.0),
                                   waits=([(sdve, sdve.cnt)] if j == 0 else []),
                                   inc=((sact, 1) if j == 3 else None))
                            for j, hh in enumerate(hhs):
                                V_(lambda v, j=j: v.tensor_tensor(
                                    cbe[j].ap, cbm.ap, esb2.s(j * 128, (j + 1) * 128),
                                    ALU.mult),
                                   waits=([(sact, sact.cnt)] if j == 0 else []),
                                   inc=((sdve, 1) if j == 3 else None))
                            wce = sdve.cnt
                            for j, hh in enumerate(hhs):
                                bnk = pb[2 + hh // 8]
                                T_(lambda t, j=j, hh=hh, bnk=bnk: t.matmul(
                                    bnk[:, (hh % 8) * P:(hh % 8) * P + P], cbe[j].ap,
                                    hdt.s(hh * P, (hh + 1) * P), start=True, stop=True),
                                   waits=([(sdve, wce)] if j == 0 else []),
                                   inc=((spe, 1) if j == 3 else None))
                        wydiag = spe.cnt
                        pb4_free = (sdve, sdve.cnt)

                    T_(lambda t, gi=gi, c=c: t.transpose(
                        pbt[:, 896:1024],
                        BC.s(gi * T + c * 128, gi * T + (c + 1) * 128), c_idb.ap),
                       waits=[pbt_free, (sact, sact.cnt), (sdve, wdt)], inc=(spe, 1))
                    S_(lambda s: s.activation(btm.ap, pbt[:, 896:1024], AF.Copy),
                       waits=[(spe, spe.cnt)], inc=(sact, 1))
                    pbt_free = (sact, sact.cnt)
                    wbtm = sact.cnt
                    for half in range(2):
                        T_(lambda t, half=half: t.matmul(
                            pb[5][:, 0:512], btm.ap,
                            hdtd.s(half * 512, (half + 1) * 512),
                            start=True, stop=True),
                           waits=[(sact, wbtm), st_free], inc=(spe, 1))
                        wst = spe.cnt
                        for h8 in range(8):
                            hh = half * 8 + h8
                            h = gi * HPG + hh
                            V_(lambda v, hh=hh, h8=h8, h=h, c=c: v.scalar_tensor_tensor(
                                P_g.s(hh * P, (hh + 1) * P), P_g.s(hh * P, (hh + 1) * P),
                                sc_a.c(c * 128 + h), pb[5][:, h8 * P:(h8 + 1) * P],
                                ALU.mult, ALU.add),
                               waits=([(spe, wst)] if h8 == 0 else []),
                               inc=((sdve, 1) if h8 == 7 else None))
                        st_free = (sdve, sdve.cnt)

                    if debug and not main and gi == 0 and c == 0:
                        Y_(lambda y: y.dma_start(d_dbg[8], P_g.ap),
                           waits=[(sdve, sdve.cnt)], inc=(qio, 16))
                        G_(lambda g: g.dma_start(d_dbg[9], hdtd.ap),
                           waits=[(sdve, sdve.cnt)], inc=(qw, 16))
                        G_(lambda g: g.dma_start(d_dbg[10][:, 0:128], btm.ap),
                           inc=(qw, 16))
                        V_(lambda v: v.tensor_copy(dtdc.c(0), dtdc.c(0)),
                           waits=[(qio, qio.cnt), (qw, qw.cnt)], inc=(sdve, 1))

                    if main:
                        for hh in range(HPG):
                            h = gi * HPG + hh
                            ob = pb[hh // 8]
                            V_(lambda v, hh=hh, h=h, c=c, ob=ob: v.tensor_scalar(
                                yA.s(hh * P, (hh + 1) * P),
                                ob[:, (hh % 8) * P:(hh % 8) * P + P],
                                sc_ecs.c(c * 128 + h), None, ALU.mult),
                               waits=([(spe, wydiag)] if hh == 0 else []))
                        for hh in range(HPG):
                            h = gi * HPG + hh
                            db = pb[2 + hh // 8]
                            V_(lambda v, hh=hh, h=h, cw=cw, db=db: v.scalar_tensor_tensor(
                                yB.s(hh * P, (hh + 1) * P),
                                hidtm.s(cw + hh * P, cw + hh * P + P),
                                c_dpbc.c(h), db[:, (hh % 8) * P:(hh % 8) * P + P],
                                ALU.mult, ALU.add))
                        for hh in range(HPG):
                            V_(lambda v, hh=hh: v.tensor_tensor(
                                yA.s(hh * P, (hh + 1) * P), yA.s(hh * P, (hh + 1) * P),
                                yB.s(hh * P, (hh + 1) * P), ALU.add),
                               inc=((sdve, 1) if hh == HPG - 1 else None))
                        y_banks_free = (sdve, sdve.cnt)
                        for ot in range(8):
                            T_(lambda t, ot=ot, c=c: t.transpose(
                                pbt[:, ot * 128:(ot + 1) * 128],
                                gfm.s(ot * T + c * 128, ot * T + (c + 1) * 128),
                                c_idb.ap),
                               waits=([pbt_free, (sdve, sdve.cnt)] if ot == 0 else []),
                               inc=((spe, 1) if ot == 7 else None))
                        wgt_ = spe.cnt
                        for ot in range(8):
                            V_(lambda v, ot=ot: v.tensor_tensor(
                                yA.s(ot * 128, (ot + 1) * 128),
                                yA.s(ot * 128, (ot + 1) * 128),
                                pbt[:, ot * 128:(ot + 1) * 128], ALU.mult),
                               waits=([(spe, wgt_)] if ot == 0 else []),
                               inc=((sdve, 1) if ot == 7 else None))
                        pbt_free = (sdve, sdve.cnt)
                        S_(lambda s: s.square(yB.ap, yA.ap),
                           waits=[(sdve, sdve.cnt)], inc=(sact, 1))
                        VE(lambda v: v.reduce_sum(gcols.c(0), yB.ap, axis=AX.X),
                           waits=[(sact, sact.cnt)])
                        VE(lambda v: v.tensor_scalar(gcols.c(1), gcols.c(0),
                                                     1.0 / CPG, EPS, ALU.mult, ALU.add))
                        S_(lambda s: s.sqrt(gcols.c(2), gcols.c(1)),
                           waits=[(sdve, sdve.cnt)], inc=(sact, 1))
                        VE(lambda v: v.reciprocal(gcols.c(3), gcols.c(2)),
                           waits=[(sact, sact.cnt)])
                        VE(lambda v: v.tensor_scalar(yB.ap, yA.ap, gcols.c(3), None,
                                                     ALU.mult))
                        wyn = sdve.cnt
                        for tb in range(2):
                            for j in range(4):
                                ot = tb * 4 + j
                                T_(lambda t, ot=ot, j=j: t.transpose(
                                    pb7[:, j * 128:(j + 1) * 128],
                                    yB.s(ot * 128, (ot + 1) * 128), c_idf.ap),
                                   waits=([(sdve, wyn), (sact, sact.cnt)] if j == 0
                                          else []),
                                   inc=((spe, 1) if j == 3 else None))
                            wyt2 = spe.cnt
                            for j in range(4):
                                ot = tb * 4 + j
                                S_(lambda s, ot=ot, gi=gi, j=j: s.activation(
                                    yst.s(ot * 128, (ot + 1) * 128),
                                    pb7[:, j * 128:(j + 1) * 128], AF.Copy,
                                    scale=c_gnw.c(gi * 8 + ot)),
                                   waits=([(spe, wyt2), yst_free] if j == 0 else []),
                                   inc=((sact, 1) if j == 3 else None))
                            for j in range(4):
                                ot = tb * 4 + j
                                Y_(lambda y, ot=ot, gi=gi, c=c: y.dma_start(
                                    d_yT[gi * 8 + ot, :, c * 128:(c + 1) * 128],
                                    yst.s(ot * 128, (ot + 1) * 128)),
                                   waits=([(sact, sact.cnt)] if j == 0 else []),
                                   inc=(qio, 16))
                        yst_free = (qio, qio.cnt)
                if not main:
                    S_(lambda s: s.activation(P_bf.ap, P_g.ap, AF.Copy),
                       waits=[(sdve, sdve.cnt), (qio, qio.cnt)], inc=(sact, 1))
                    Y_(lambda y, gi=gi: y.dma_start(d_ppro[gi], P_bf.ap),
                       waits=[(sact, sact.cnt)], inc=(qio, 16))

            hidtm.free(); hfm.free(); P_bf.free()
            for b in cbe:
                b.free()
            btm.free(); hdtd.free(); cbm.free(); esb.free(); esb2.free()
            if dtdc is not None:
                dtdc.free()
            gcols.free()
            if main:
                gfm.free(); hdt.free(); yst.free()
            BC.free()
            ovl.free()
            wsl.free()
            dtT.free(); sc_dt.free(); sc_dec.free(); sc_a.free(); P_g.free()
            sc_ncs.free()
            if main:
                sc_ecs.free()
            hT.free()

            if not main:
                return

            yT = ABF.alloc(64 * T)
            wsl3 = ABF.alloc(64 * 128)
            xnf = [AFA.alloc(1024) for _ in range(2)]
            s_sb = AFA.alloc(1024)
            Y_(None, waits=[(qio, qio.cnt)])
            for kt in range(64):
                Y_(lambda y, kt=kt: y.dma_start(yT.s(kt * T, (kt + 1) * T), d_yT[kt]),
                   waits=(allw() if kt == 0 else []), inc=(qio, 16))
            wyt_all = qio.cnt
            sfm_free = (qo, qo.cnt)
            xnf_free = [allw(), allw()]
            for ot in range(32):
                G_(lambda g, ot=ot: g.dma_start(wsl3.ap, wot[ot]),
                   waits=[(spe, spe.cnt)], inc=(qw, 16))
                wqw = qw.cnt
                Y_(lambda y, ot=ot, b=xnf[ot % 2]: y.dma_start(b.ap, d_xnTf[ot]),
                   waits=list(xnf_free[ot % 2]), inc=(qio, 16))
                wxn = qio.cnt
                for nb in range(2):
                    bank = pb[nb]
                    for kt in range(64):
                        st = (kt == 0)
                        sp = (kt == 63)
                        ws = [(qw, wqw), (qio, wyt_all), bank_free[nb]] if st else []
                        T_(lambda t, bank=bank, kt=kt, nb=nb, st=st, sp=sp:
                           t.matmul(bank[:, 0:512], wsl3.s(kt * 128, (kt + 1) * 128),
                                    yT.s(kt * T + nb * 512, kt * T + nb * 512 + 512),
                                    start=st, stop=sp),
                           waits=ws, inc=((spe, 1) if sp else None))
                    wmm = spe.cnt
                    V_(lambda v, nb=nb, bank=bank, b=xnf[ot % 2]: v.tensor_tensor(
                        s_sb.s(nb * 512, (nb + 1) * 512), b.s(nb * 512, (nb + 1) * 512),
                        bank[:, 0:512], ALU.add),
                       waits=[(spe, wmm), (qio, wxn), sfm_free],
                       inc=(sdve, 1))
                    bank_free[nb] = (sdve, sdve.cnt)
                xnf_free[ot % 2] = [(sdve, sdve.cnt)]
                Y_(lambda y, ot=ot: y.dma_start(d_sfm[ot], s_sb.ap),
                   waits=[(sdve, sdve.cnt)], inc=(qo, 16))
                sfm_free = (qo, qo.cnt)
            sfm_all = qo.cnt
            yT.free(); wsl3.free()
            for b in xnf:
                b.free()
            s_sb.free()

            xnb = ABF.alloc(32 * T)
            Y_(None, waits=[(qio, qio.cnt)])
            for kt in range(32):
                Y_(lambda y, kt=kt: y.dma_start(xnb.s(kt * T, (kt + 1) * T), d_xnTb[kt]),
                   waits=(allw() if kt == 0 else []), inc=(qio, 16))
            wxnb = qio.cnt
            asl = ABF.alloc(2048)
            rslf = AFA.alloc(512)
            xnf32 = AFA.alloc(4096)
            G_(lambda g: g.dma_start(asl.ap, arw), waits=allw(), inc=(qw, 16))
            Y_(lambda y: y.dma_start(rslf.ap, rww), waits=allw(), inc=(qio, 16))
            wslq = qw.cnt
            GT = ABF.alloc(8 * T)
            gtb = ABF.alloc(1024)
            lsb = AFA.alloc(128)
            ltm = AFA.alloc(16)
            rsb = AFA.alloc(128)
            rtm = AFA.alloc(64)
            gtm = AFA.alloc(1024)
            comb = AFA.alloc(16)
            fcols = AFA.alloc(8)
            G_(lambda g: g.memset(lsb.ap, 0.0), waits=allw())
            G_(lambda g: g.memset(rsb.ap, 0.0), inc=(sgp, 1))
            wms = sgp.cnt
            pbt_free2 = (sdve, sdve.cnt)

            xnf_free = (sdve, sdve.cnt)
            for tt in range(NTT):
                for kt in range(32):
                    Y_(lambda y, kt=kt, tt=tt: y.dma_start(
                        xnf32.s(kt * 128, (kt + 1) * 128),
                        d_xnTf[kt, :, tt * 128:(tt + 1) * 128]),
                       waits=([xnf_free, (qio, qio.cnt)] if kt == 0 else []),
                       inc=(qio, 16))
                wxf = qio.cnt
                for kt in range(32):
                    st = (kt == 0)
                    sp = (kt == 31)
                    T_(lambda t, kt=kt, tt=tt, st=st, sp=sp: t.matmul(
                        pb[4][0:16, 0:128], rslf.s(kt * 16, (kt + 1) * 16),
                        xnf32.s(kt * 128, (kt + 1) * 128),
                        start=st, stop=sp),
                       waits=([(qio, wxf), (qw, wslq), (sact, sact.cnt),
                               (sdve, sdve.cnt)] if st else []),
                       inc=((spe, 1) if sp else None))
                xnf_free = (spe, spe.cnt)
                S_(lambda s: s.activation(lsb.p(0, 16, 0, 128), pb[4][0:16, 0:128],
                                          AF.Copy),
                   waits=[(spe, spe.cnt), (sgp, wms)], inc=(sact, 1))
                T_(lambda t: t.transpose(pb7[:, 0:128], lsb.ap, c_idf.ap),
                   waits=[(sact, sact.cnt), (sdve, sdve.cnt)], inc=(spe, 1))
                S_(lambda s: s.activation(ltm.ap, pb7[:, 0:16], AF.Copy),
                   waits=[(spe, spe.cnt)], inc=(sact, 1))
                VE(lambda v: v.reduce_max(fcols.c(0), ltm.ap, axis=AX.X),
                   waits=[(sact, sact.cnt)])
                VE(lambda v: v.tensor_scalar(fcols.c(1), fcols.c(0), -1.0, None,
                                             ALU.mult))
                S_(lambda s: s.activation(ltm.ap, ltm.ap, AF.Exp, bias=fcols.c(1)),
                   waits=[(sdve, sdve.cnt)], inc=(sact, 1))
                VE(lambda v: v.reduce_sum(fcols.c(2), ltm.ap, axis=AX.X),
                   waits=[(sact, sact.cnt)])
                VE(lambda v: v.reciprocal(fcols.c(3), fcols.c(2)))
                VE(lambda v: v.tensor_scalar(ltm.ap, ltm.ap, fcols.c(3), None, ALU.mult))
                VE(lambda v: v.reduce_max(fcols.c(4), ltm.ap, axis=AX.X))
                VE(lambda v: v.tensor_scalar(comb.ap, ltm.ap, fcols.c(4), None,
                                             ALU.is_ge))
                VE(lambda v: v.scalar_tensor_tensor(comb.ap, comb.ap, -1e30, ltm.ap,
                                                    ALU.mult, ALU.add))
                VE(lambda v: v.reduce_max(fcols.c(5), comb.ap, axis=AX.X))
                VE(lambda v: v.tensor_scalar(comb.ap, ltm.ap, fcols.c(5), None,
                                             ALU.is_ge))
                VE(lambda v: v.tensor_tensor(comb.ap, comb.ap, ltm.ap, ALU.mult))
                VE(lambda v: v.tensor_tensor(fcols.c(6), fcols.c(4), fcols.c(5),
                                             ALU.add))
                VE(lambda v: v.tensor_scalar(fcols.c(6), fcols.c(6), 1e-6, None,
                                             ALU.add))
                VE(lambda v: v.reciprocal(fcols.c(7), fcols.c(6)))
                VE(lambda v: v.tensor_scalar(comb.ap, comb.ap, fcols.c(7), 0.5,
                                             ALU.mult, ALU.mult))
                wcomb = sdve.cnt
                for kt in range(32):
                    st = (kt == 0)
                    sp = (kt == 31)
                    T_(lambda t, kt=kt, tt=tt, st=st, sp=sp: t.matmul(
                        pb[5][0:64, 0:128], asl.s(kt * 64, (kt + 1) * 64),
                        xnb.s(kt * T + tt * 128, kt * T + (tt + 1) * 128),
                        start=st, stop=sp),
                       waits=([(sact, sact.cnt), (sdve, sdve.cnt)] if st else []),
                       inc=((spe, 1) if sp else None))
                S_(lambda s: s.activation(rsb.p(0, 64, 0, 128), pb[5][0:64, 0:128],
                                          AF.Copy),
                   waits=[(spe, spe.cnt)], inc=(sact, 1))
                T_(lambda t: t.transpose(pb7[:, 128:256], rsb.ap, c_idf.ap),
                   waits=[(sact, sact.cnt)], inc=(spe, 1))
                S_(lambda s: s.activation(rtm.ap, pb7[:, 128:192], AF.Copy),
                   waits=[(spe, spe.cnt)], inc=(sact, 1))
                for e in range(16):
                    V_(lambda v, e=e: v.tensor_scalar(
                        gtm.s(e * 64, (e + 1) * 64), rtm.ap, comb.c(e), None, ALU.mult),
                       waits=([(sact, sact.cnt), (sdve, wcomb)] if e == 0 else []),
                       inc=((sdve, 1) if e == 15 else None))
                S_(lambda s: s.activation(gtb.ap, gtm.ap, AF.Copy),
                   waits=[(sdve, sdve.cnt)], inc=(sact, 1))
                for rt in range(8):
                    T_(lambda t, rt=rt: t.transpose(
                        pbt[:, rt * 128:(rt + 1) * 128], gtb.s(rt * 128, (rt + 1) * 128),
                        c_idb.ap),
                       waits=([(sact, sact.cnt), pbt_free2] if rt == 0 else []),
                       inc=((spe, 1) if rt == 7 else None))
                wgt2 = spe.cnt
                for rt in range(8):
                    V_(lambda v, rt=rt, tt=tt: v.tensor_copy(
                        GT.s(rt * T + tt * 128, rt * T + (tt + 1) * 128),
                        pbt[:, rt * 128:(rt + 1) * 128]),
                       waits=([(spe, wgt2)] if rt == 0 else []),
                       inc=((sdve, 1) if rt == 7 else None))
                pbt_free2 = (sdve, sdve.cnt)
            lsb.free(); ltm.free(); rsb.free(); rtm.free(); gtm.free(); comb.free()
            fcols.free(); gtb.free()

            wsl4 = ABF.alloc(32 * 128)
            bfsl = ABF.alloc(8 * 128)
            g_sb = AFA.alloc(1024)
            s2 = [AFA.alloc(1024) for _ in range(2)]
            sg = AFA.alloc(1024)
            xb = [AFA.alloc(128) for _ in range(2)]
            ob = [AFA.alloc(128) for _ in range(2)]
            s2_free = [(sdve, sdve.cnt), (sdve, sdve.cnt)]
            xb_free = [(sdve, sdve.cnt), (sdve, sdve.cnt)]
            ob_free = [(qo, qo.cnt), (qo, qo.cnt)]
            pb7_free2 = [(sdve, sdve.cnt)]
            for ot in range(32):
                G_(lambda g, ot=ot: g.dma_start(wsl4.ap, wbt[ot]),
                   waits=[(spe, spe.cnt)], inc=(qw, 16))
                G_(lambda g, ot=ot: g.dma_start(bfsl.ap, bft[ot]), inc=(qw, 16))
                wqw = qw.cnt
                Y_(lambda y, ot=ot, b=s2[ot % 2]: y.dma_start(b.ap, d_sfm[ot]),
                   waits=[s2_free[ot % 2], (qo, sfm_all)], inc=(qio, 16))
                ws2 = qio.cnt
                for nb in range(2):
                    bank = pb[nb]
                    for kt in range(32):
                        st = (kt == 0)
                        ws = [(qw, wqw), bank_free[nb]] if st else []
                        T_(lambda t, bank=bank, kt=kt, nb=nb, st=st: t.matmul(
                            bank[:, 0:512], wsl4.s(kt * 128, (kt + 1) * 128),
                            xnb.s(kt * T + nb * 512, kt * T + nb * 512 + 512),
                            start=st, stop=False), waits=ws)
                    for rt in range(8):
                        sp = (rt == 7)
                        T_(lambda t, bank=bank, rt=rt, nb=nb, sp=sp: t.matmul(
                            bank[:, 0:512], bfsl.s(rt * 128, (rt + 1) * 128),
                            GT.s(rt * T + nb * 512, rt * T + nb * 512 + 512),
                            start=False, stop=sp),
                           inc=((spe, 1) if sp else None))
                    wmm = spe.cnt
                    S_(lambda s, bank=bank, nb=nb, ot=ot: s.activation(
                        g_sb.s(nb * 512, (nb + 1) * 512), bank[:, 0:512], AF.Silu,
                        bias=c_bb.c(ot)),
                       waits=[(spe, wmm)], inc=(sact, 1))
                    bank_free[nb] = (sact, sact.cnt)
                V_(lambda v, b=s2[ot % 2]: v.tensor_tensor(sg.ap, b.ap, g_sb.ap,
                                                           ALU.mult),
                   waits=[(sact, sact.cnt), (qio, ws2)], inc=(sdve, 1))
                s2_free[ot % 2] = (sdve, sdve.cnt)
                wsg = sdve.cnt
                for tb in range(2):
                    for j in range(4):
                        tt = tb * 4 + j
                        T_(lambda t, tt=tt, j=j: t.transpose(
                            pb7[:, j * 128:(j + 1) * 128],
                            sg.s(tt * 128, (tt + 1) * 128), c_idf.ap),
                           waits=([(sdve, wsg)] + pb7_free2 if j == 0 else []),
                           inc=((spe, 1) if j == 3 else None))
                    wtp = spe.cnt
                    for j in range(4):
                        tt = tb * 4 + j
                        bx = xb[tt % 2]
                        bo = ob[tt % 2]
                        Y_(lambda y, tt=tt, ot=ot, bx=bx: y.dma_start(
                            bx.ap,
                            x_own[tt * 128:(tt + 1) * 128, ot * 128:(ot + 1) * 128]),
                           waits=[xb_free[tt % 2]], inc=(qio, 16))
                        wxb = qio.cnt
                        V_(lambda v, bx=bx, bo=bo, j=j: v.tensor_tensor(
                            bo.ap, bx.ap, pb7[:, j * 128:(j + 1) * 128], ALU.add),
                           waits=[(spe, wtp), (qio, wxb), ob_free[tt % 2]],
                           inc=(sdve, 1))
                        xb_free[tt % 2] = (sdve, sdve.cnt)
                        Y_(lambda y, tt=tt, ot=ot, bo=bo: y.dma_start(
                            out_t[tt * 128:(tt + 1) * 128, ot * 128:(ot + 1) * 128],
                            bo.ap),
                           waits=[(sdve, sdve.cnt)], inc=(qo, 16))
                        ob_free[tt % 2] = (qo, qo.cnt)
                    pb7_free2 = [(sdve, sdve.cnt)]
            xnb.free(); asl.free(); rslf.free(); xnf32.free()
            GT.free(); wsl4.free(); bfsl.free()
            g_sb.free(); sg.free()
            for b in s2 + xb + ob:
                b.free()

        emit_pass(x_pro, main=False)
        emit_pass(x_own, main=True)
        Y_(None, waits=[(qo, qo.cnt)])

        with nc.Block() as blk:
            @blk.tensor
            def _(t):
                for f in pr.q["T"]:
                    f(t)

            @blk.vector
            def _(v):
                for f in pr.q["V"]:
                    f(v)

            @blk.scalar
            def _(s):
                for f in pr.q["S"]:
                    f(s)

            @blk.gpsimd
            def _(g):
                for f in pr.q["G"]:
                    f(g)

            @blk.sync
            def _(y):
                for f in pr.q["Y"]:
                    f(y)

    return nc, {k: len(v) for k, v in pr.q.items()}


_prog_cache = {}


def _tile_lhsT(w, ntile_o, ntile_k):
    O, K = w.shape
    wr = w.reshape(ntile_o, 128, ntile_k, 128)
    return np.ascontiguousarray(wr.transpose(0, 3, 2, 1).reshape(ntile_o, 128, K))


def kernel(L0_attn_x, L0_attn_norm_w, L0_attn_in_proj_w, L0_attn_conv_w,
           L0_attn_conv_b, L0_attn_dt_bias, L0_attn_A_log, L0_attn_Dp,
           L0_attn_gnorm_w, L0_attn_out_proj_w, L0_ffn_base_w, L0_ffn_base_b,
           L0_ffn_A, L0_ffn_B, L0_ffn_router_w):
    x = np.ascontiguousarray(np.asarray(L0_attn_x, dtype=np.float32))
    ipw = np.asarray(L0_attn_in_proj_w, dtype=np.float32)
    conv_w = np.asarray(L0_attn_conv_w, dtype=np.float32)
    conv_b = np.asarray(L0_attn_conv_b, dtype=np.float32)
    dt_bias = np.asarray(L0_attn_dt_bias, dtype=np.float32)
    A_log = np.asarray(L0_attn_A_log, dtype=np.float32)
    Dp = np.asarray(L0_attn_Dp, dtype=np.float32)
    gnorm_w = np.asarray(L0_attn_gnorm_w, dtype=np.float32)
    norm_w = np.asarray(L0_attn_norm_w, dtype=np.float32)
    opw = np.asarray(L0_attn_out_proj_w, dtype=np.float32)
    bw = np.asarray(L0_ffn_base_w, dtype=np.float32)
    bb = np.asarray(L0_ffn_base_b, dtype=np.float32)
    Al = np.asarray(L0_ffn_A, dtype=np.float32)
    Bl = np.asarray(L0_ffn_B, dtype=np.float32)
    rw = np.asarray(L0_ffn_router_w, dtype=np.float32)

    W1 = np.concatenate([ipw[18432:18560], ipw[16384:17408], ipw[17408:18432]], 0)
    w1t = _tile_lhsT(W1, 17, 32)
    wht = _tile_lhsT(ipw[8192:16384], 64, 32)
    wgt = _tile_lhsT(ipw[0:8192], 64, 32)
    wot = _tile_lhsT(opw, 32, 64)
    wbt = _tile_lhsT(bw, 32, 32)
    BF_ = np.ascontiguousarray(Bl.transpose(0, 2, 1).reshape(1024, 4096))
    bft = _tile_lhsT(np.ascontiguousarray(BF_.T), 32, 8)
    arw = np.ascontiguousarray(Al.reshape(64, 32, 128).transpose(2, 1, 0).reshape(128, 2048))
    rww = np.ascontiguousarray(rw.reshape(16, 32, 128).transpose(2, 1, 0).reshape(128, 512))

    def fm(vec, ntile):
        return np.ascontiguousarray(vec.reshape(ntile, 128).T)

    cwB = conv_w[8192:9216].reshape(8, 128, KC)
    cwC = conv_w[9216:10240].reshape(8, 128, KC)
    cwB1 = np.ascontiguousarray(
        np.concatenate([cwB, cwC], 0).transpose(1, 0, 2).reshape(128, 16 * KC))
    cbB1 = np.ascontiguousarray(
        np.concatenate([conv_b[8192:9216].reshape(8, 128),
                        conv_b[9216:10240].reshape(8, 128)], 0).T)
    cwH = np.ascontiguousarray(
        conv_w[0:8192].reshape(64, 128, KC).transpose(1, 0, 2).reshape(128, 64 * KC))
    cbH = fm(conv_b[0:8192], 64)
    gnw = fm(gnorm_w, 64)
    nww = fm(norm_w, 32)
    bbw = fm(bb, 32)

    import os
    dbg = bool(os.environ.get("KDBG"))
    key = "v1dbg" if dbg else "v1"
    if key not in _prog_cache:
        _prog_cache[key] = build(debug=dbg)[0]
    nc = _prog_cache[key]

    shared = dict(
        w1t=w1t, wht=wht, wgt=wgt, wot=wot, wbt=wbt, bft=bft, arw=arw, rww=rww,
        cwB1=cwB1, cbB1=cbB1, cwH=cwH, cbH=cbH, gnw=gnw, nww=nww, bbw=bbw,
        dtb=np.ascontiguousarray(dt_bias.reshape(128, 1)),
        alog=np.ascontiguousarray(A_log.reshape(128, 1)),
        dprow=np.ascontiguousarray(Dp.reshape(1, 128)),
        idf=np.eye(128, dtype=np.float32),
        idb=np.eye(128).astype(BFD),
        trif=np.triu(np.ones((128, 128), dtype=np.float32)),
        onesf=np.ones((1, 128), dtype=np.float32),
    )
    in_maps = []
    for core in range(8):
        b, k = core // 2, core % 2
        m = dict(shared)
        m["x_own"] = np.ascontiguousarray(x[b, k * T:(k + 1) * T])
        m["x_pro"] = np.ascontiguousarray(x[b, 0:T])
        m["flagc"] = np.full((128, 1), float(k), dtype=np.float32)
        in_maps.append(m)

    res = run_bass_kernel_spmd(nc, in_maps, core_ids=list(range(8)))
    if dbg:
        global LAST_RES
        LAST_RES = res.results
    out = np.empty((4, 2048, D), dtype=np.float32)
    for core in range(8):
        b, k = core // 2, core % 2
        out[b, k * T:(k + 1) * T] = res.results[core]["out"]
    return out


if __name__ == "__main__":
    nc, counts = build()
    print("built ok", counts)
